# version 15
# baseline (speedup 1.0000x reference)
import numpy as np
import ml_dtypes
import concourse.bass as bass
import concourse.mybir as mybir
from concourse.bass_utils import run_bass_kernel_spmd

F32 = mybir.dt.float32
BF16 = mybir.dt.bfloat16
I8 = mybir.dt.int8
AF = mybir.ActivationFunctionType
ALU = mybir.AluOpType
AX = mybir.AxisListType

NCORES = 8
T_TOK = 2048
D = 1024
NT = 16
NST = 4
LN8 = 2.0794415416798357


class Sched:

    CENGS = ("T", "V", "S")
    ENGS = ("T", "V", "S", "Q", "G")

    def __init__(self, nc):
        self.nc = nc
        self.sems = {e: nc.alloc_semaphore(f"sem{e}") for e in self.CENGS}
        self.count = {e: 0 for e in self.CENGS}
        self.keys = {}
        self.ops = {e: [] for e in self.ENGS}
        self.ndma = 0

    def add(self, eng, fn, waits=(), key=None, dma=False, fence=False):
        for w in waits:
            assert w in self.keys, f"wait on unknown key {w}"
        if key is not None:
            assert key not in self.keys, f"dup key {key}"
        if dma:
            assert key is not None, "every DMA needs a key/semaphore"
            sem = self.nc.alloc_semaphore(f"dma{self.ndma}")
            self.ndma += 1
            self.keys[key] = ("dma", sem)
            self.ops[eng].append((fn, tuple(self.keys[w] for w in waits), sem, 0))
        else:
            inc = 1 if key is not None else 0
            assert not (fence and not inc), "fence requires a key"
            if inc:
                self.count[eng] += inc
                self.keys[key] = ("eng", eng, self.count[eng], fence)
            self.ops[eng].append((fn, tuple(self.keys[w] for w in waits), None, inc))

    def dma_sem(self, key):
        kind, sem = self.keys[key][0], self.keys[key][1]
        assert kind == "dma"
        return sem

    def emit_engine(self, e, eng):
        hw = {x: 0 for x in self.CENGS}
        dma_seen = set()
        for fn, waits, dsem, inc in self.ops[e]:
            for w in waits:
                if w[0] == "eng":
                    _, we, wv, wfence = w
                    if we == e and not wfence:
                        continue
                    if wv > hw[we]:
                        eng.wait_ge(self.sems[we], wv)
                        hw[we] = wv
                else:
                    sem = w[1]
                    if id(sem) not in dma_seen:
                        eng.wait_ge(sem, 16)
                        dma_seen.add(id(sem))
            ins = fn(eng)
            if dsem is not None:
                assert ins is not None, "dma op returned no instr"
                ins.then_inc(dsem, 16)
            elif inc:
                assert ins is not None, "op with inc returned no instr"
                ins.then_inc(self.sems[e], inc)


def build_program():
    nc = bass.Bass("TRN2", target_bir_lowering=False, debug=False,
                   detect_race_conditions=False,
                   dynamic_dma_scratch_size=16384)

    dx = nc.dram_tensor("x", [128, NT * D], F32, kind="ExternalInput").ap()
    dwqT = nc.dram_tensor("wqT", [128, 8 * D], F32, kind="ExternalInput").ap()
    dwkT = nc.dram_tensor("wkT", [128, 8 * D], F32, kind="ExternalInput").ap()
    dwvT = nc.dram_tensor("wvT", [128, 8 * D], F32, kind="ExternalInput").ap()
    dbq = nc.dram_tensor("bq", [128, 8], F32, kind="ExternalInput").ap()
    dbk = nc.dram_tensor("bk", [128, 8], F32, kind="ExternalInput").ap()
    dbv = nc.dram_tensor("bv", [D], F32, kind="ExternalInput").ap()
    dwblk = nc.dram_tensor("wblk", [128, 8 * 128], F32, kind="ExternalInput").ap()
    dwffn = nc.dram_tensor("wffn", [8, 128, D], F32, kind="ExternalInput").ap()
    dhad = nc.dram_tensor("had", [128, 8 * D], BF16, kind="ExternalInput").ap()
    dones_s = nc.dram_tensor("ones_s", [128, 128], BF16, kind="ExternalInput").ap()
    did_bf = nc.dram_tensor("id_bf", [128, 128], BF16, kind="ExternalInput").ap()
    did16 = nc.dram_tensor("id16_f32", [16, 16], F32, kind="ExternalInput").ap()
    dout = nc.dram_tensor("out", [128, NT * D], F32, kind="ExternalOutput").ap()
    dtern = nc.dram_tensor("dtern", [D, D], BF16, kind="Internal").ap()

    sb = lambda n, s, d=F32: nc.alloc_sbuf_tensor(n, s, d).ap()
    x_sb = sb("x_sb", [128, NT * D])
    wqT = sb("wqT_sb", [128, 8 * D], BF16)
    wkT = sb("wkT_sb", [128, 8 * D], BF16)
    wvT = sb("wvT_sb", [128, 8 * D], BF16)
    q_sb = sb("q_sb", [128, 8 * T_TOK], BF16)
    k_sb = sb("k_sb", [128, 8 * T_TOK], BF16)
    had_sb = k_sb[:, 0:8 * D]
    wblk = sb("wblk_sb", [128, 8 * 128], BF16)
    ones_s = sb("ones_s_sb", [128, 128], BF16)
    id_bf = sb("id_bf_sb", [128, 128], BF16)
    id16_f32 = sb("id16_sb", [16, 16], F32)
    bq_c = sb("bq_c", [128, 8])
    bk_c = sb("bk_c", [128, 8])
    bv_bf = sb("bv_bf", [1, D], BF16)
    ones1_bf = sb("ones1_bf", [1, 128], BF16)
    ones128_f = sb("ones128_f", [128, 1], F32)
    eps_c = sb("eps_c", [128, 1])
    negm_s = sb("negm_s", [128, NT])
    rstd_s = sb("rstd_s", [128, NT])
    bn6 = sb("bn6", [128, 12])
    bn2_b = [sb(f"bn2_{i}", [128, 2]) for i in range(2)]
    stat1 = sb("stat1", [128, 1])
    lnT_st = sb("lnT_st", [128, 8 * 512], BF16)
    xc_b = [sb(f"xc{i}", [128, D], BF16) for i in range(2)]
    diag_b = [sb(f"diag{i}", [128, 128], BF16) for i in range(2)]
    lnT_b = [sb(f"lnT{i}", [128, 8 * 128], BF16) for i in range(2)]
    qT_b = lnT_b
    qbf_b = xc_b
    qi8_1 = sb("qi8", [128, D], I8)
    qi8_b = [qi8_1, qi8_1]
    phiq_b = [lnT_st[:, 512 * i:512 * (i + 1)] for i in range(2)]
    phik_b = [lnT_st[:, 1024 + 512 * i:1024 + 512 * (i + 1)] for i in range(2)]
    p_b = [lnT_st[:, 2048 + 512 * i:2048 + 512 * (i + 1)] for i in range(2)]
    rm_c = sb("rm_c", [128, 1])
    cc_c = sb("cc_c", [128, 1])
    rs_b = [sb(f"rs{i}", [128, 1]) for i in range(2)]
    scale_b = [sb(f"scale{i}", [128, 1]) for i in range(2)]
    sqp = sb("sqp", [128, 8])
    sqf = sb("sqf", [128, 2])
    sq2 = sb("sq2", [1, 2], BF16)
    sqb = sb("sqb", [128, 2])
    rq2 = sb("rq2", [128, 2])
    sqn = sb("sqn", [128, 2])
    lnS0 = sb("lnS0", [128, 2])
    tt_c = sb("tt_c", [128, 2])
    swcol = sb("swcol", [128, 8])
    swsc = sb("swsc", [128, 8])
    swrcp = sb("swrcp", [128, 8])
    arena = sb("arena", [128, 8192], mybir.dt.uint8)
    junk_bf = arena.bitcast(BF16)
    sg_sc = arena.bitcast(F32)[:, 1024:2048]
    e_fm = arena.bitcast(F32)[0:16, 0:512]
    e_tm = arena.bitcast(F32)[:, 1024:1088]
    e_rc = arena.bitcast(F32)[0:16, 512:1024]
    d_sc = arena.bitcast(F32)[:, 0:1024]
    sg_bf = arena.bitcast(BF16)[:, 2048:3072]
    sg_bf2 = arena.bitcast(BF16)[:, 3072:4096]
    d2_sc = q_sb.bitcast(F32)[:, 0:1024]
    cjunk = q_sb.bitcast(F32)[:, 1024:2048]
    d_b = [d_sc, d2_sc]
    qf32 = q_sb.bitcast(F32)
    wst_b = [qf32[:, 1024 * i:1024 * (i + 1)] for i in range(2)]
    nwork_b = [qf32[:, 2048 + 1024 * i:2048 + 1024 * (i + 1)] for i in range(2)]
    tern_b = [q_sb[:, 8192 + 1024 * i:8192 + 1024 * (i + 1)] for i in range(2)]

    ps = lambda n, s: nc.alloc_psum_tensor(n, s, F32).ap()
    pool_a = ps("pool_a", [128, 1024])
    pool_b = ps("pool_b", [128, 1024])
    pqk_a = ps("pqk_a", [128, 512])
    pqk_b = ps("pqk_b", [128, 512])
    p_s = ps("p_s", [16, 512])
    p_misc = ps("p_misc", [128, 512])

    sch = Sched(nc)
    add = sch.add

    for st in range(NST):
        add("Q", lambda e, st=st: e.dma_start(
            out=x_sb[:, st * 4 * D:(st + 1) * 4 * D],
            in_=dx[:, st * 4 * D:(st + 1) * 4 * D]),
            dma=True, key=f"x{st}")
    add("Q", lambda e: e.dma_start(out=id_bf, in_=did_bf), dma=True, key="idb")
    add("Q", lambda e: e.dma_start(out=id16_f32, in_=did16), dma=True, key="idf")
    add("Q", lambda e: e.dma_start(out=ones_s, in_=dones_s), dma=True, key="ones_s")
    add("Q", lambda e: e.dma_start(out=bq_c, in_=dbq), dma=True, key="bq")
    add("Q", lambda e: e.dma_start(out=bk_c, in_=dbk), dma=True, key="bk")

    def add_wffn(j, waits=()):
        add("Q", lambda e, j=j: e.dma_start(
            out=wst_b[j % 2], in_=dwffn[j]),
            dma=True, key=f"wffn{j}", waits=waits)
    add_wffn(0)
    add_wffn(1)
    for nm, ddst, dsrc in (("wq", wqT, dwqT), ("wk", wkT, dwkT), ("wv", wvT, dwvT)):
        add("G", lambda e, ddst=ddst, dsrc=dsrc: e.dma_start(
            out=ddst, in_=dsrc), dma=True, key=nm)
    add("G", lambda e: e.dma_start(out=wblk, in_=dwblk), dma=True, key="wblk")
    add("G", lambda e: e.dma_start(out=bv_bf, in_=dbv[None, :]), dma=True,
        key="bvbf")

    add("V", lambda e: e.memset(ones1_bf, 1.0))
    add("V", lambda e: e.memset(ones128_f, 1.0))
    add("V", lambda e: e.memset(eps_c, 1e-5), key="konst")

    for j in range(8):
        w = wst_b[j % 2]
        nwk = nwork_b[j % 2]
        add("V", lambda e, w=w, j=j: e.tensor_reduce(
            swcol[:, j:j + 1], w, axis=AX.X, op=ALU.add,
            apply_absolute_value=True), waits=(f"wffn{j}",),
            key=f"swred{j}", fence=True)
        add("V", lambda e, j=j: e.tensor_scalar(
            swsc[:, j:j + 1], swcol[:, j:j + 1], 1.0 / 1024.0, 1e-5,
            ALU.mult, ALU.max), waits=(f"swred{j}",), key=f"swsc{j}",
            fence=True)
        add("V", lambda e, j=j: e.reciprocal(swrcp[:, j:j + 1], swsc[:, j:j + 1]),
            waits=(f"swsc{j}",), key=f"swr{j}")
        add("S", lambda e, w=w, nwk=nwk, j=j: e.activation(
            nwk, w, AF.Copy, scale=swrcp[:, j:j + 1]),
            waits=(f"swr{j}",), key=f"nq{j}")
        add("V", lambda e, nwk=nwk, j=j: e.tensor_scalar(
            sg_sc, nwk, -0.5, None, ALU.is_lt), waits=(f"nq{j}",),
            key=f"sgp{j}", fence=True)
        add("V", lambda e, nwk=nwk, j=j: e.scalar_tensor_tensor(
            tern_b[j % 2], nwk, 0.5, sg_sc, ALU.is_gt, ALU.subtract),
            waits=(f"sgp{j}",), key=f"ternq{j}", fence=True)
        add("V", lambda e, j=j: e.tensor_scalar(
            tern_b[j % 2], tern_b[j % 2], swsc[:, j:j + 1], None, ALU.mult),
            waits=(f"ternq{j}",), key=f"tern{j}")
        def trn(e, j=j):
            last = None
            for c in range(8):
                last = e.matmul(pool_a[:, 128 * c:128 * (c + 1)],
                                tern_b[j % 2][:, 128 * c:128 * (c + 1)],
                                id_bf, start=True, stop=True)
            return last
        add("T", trn,
            waits=(f"tern{j}", "idb") + ((f"ternev{j - 1}",) if j else ()),
            key=f"ternmm{j}")
        add("S", lambda e, j=j: e.activation(
            lnT_b[j % 2], pool_a, AF.Copy),
            waits=(f"ternmm{j}",) + ((f"terndo{j - 2}",) if j >= 2 else ()),
            key=f"ternev{j}")
        add("G", lambda e, j=j: e.dma_start(
            out=dtern.rearrange("(c p) o -> p c o", p=128)[:, :, 128 * j:128 * (j + 1)],
            in_=lnT_b[j % 2].rearrange("p (c m) -> p c m", c=8)),
            waits=(f"ternev{j}",), dma=True, key=f"terndo{j}")
        add("S", lambda e: e.nop(), waits=(f"nq{j}",), key=f"ternrd{j}")
        if j + 2 < 8:
            add_wffn(j + 2, waits=(f"ternrd{j}",))

    for st in range(NST):
        for t in range(4):
            g = st * 4 + t
            xg = x_sb[:, g * D:(g + 1) * D]
            pool = pool_a if t % 2 == 0 else pool_b
            bn2 = bn2_b[t % 2]
            add("V", lambda e, xg=xg, bn2=bn2: e.scalar_tensor_tensor(
                d_sc, xg, 1.0, xg, ALU.mult, ALU.max, accum_out=bn2[:, 0:1]),
                waits=(f"x{st}",) + ((f"xcd{g - 2}",) if g >= 2 else ()),
                key=f"sum{g}", fence=True)
            add("V", lambda e, xg=xg, bn2=bn2: e.scalar_tensor_tensor(
                d_sc, xg, 1.0, xg, ALU.mult, ALU.mult, accum_out=bn2[:, 1:2]),
                key=f"ss{g}", fence=True)
            add("V", lambda e, g=g, bn2=bn2: e.tensor_scalar_mul(
                negm_s[:, g:g + 1], bn2[:, 0:1], -1.0 / D), waits=(f"sum{g}",),
                key=f"nm{g}", fence=True)
            add("V", lambda e, g=g, bn2=bn2: e.tensor_tensor(
                bn6[:, 0:1], bn2[:, 0:1], negm_s[:, g:g + 1], ALU.mult),
                waits=(f"nm{g}",), key=f"t2{g}", fence=True)
            add("V", lambda e, bn2=bn2: e.tensor_tensor(
                bn2[:, 1:2], bn2[:, 1:2], bn6[:, 0:1], ALU.add),
                waits=(f"ss{g}", f"t2{g}"), key=f"st{g}")
            add("S", lambda e, bn2=bn2: e.activation(
                stat1, bn2[:, 1:2], AF.Ln, bias=eps_c, scale=1.0 / D),
                waits=(f"st{g}", "konst"))
            add("S", lambda e, g=g: e.activation(
                rstd_s[:, g:g + 1], stat1, AF.Exp, scale=-0.5))
            add("S", lambda e, xg=xg, t=t, g=g: e.activation(
                xc_b[t % 2], xg, AF.Identity, bias=negm_s[:, g:g + 1]))
            add("S", lambda e, t=t, g=g: e.activation(
                diag_b[t % 2], id_bf, AF.Copy, scale=rstd_s[:, g:g + 1]),
                waits=("idb",), key=f"xcd{g}")
            def lntmm(e, t=t, pool=pool):
                last = None
                for c in range(8):
                    last = e.matmul(pool[:, 128 * c:128 * (c + 1)],
                                    xc_b[t % 2][:, 128 * c:128 * (c + 1)],
                                    diag_b[t % 2], start=True, stop=True)
                return last
            wts = [f"xcd{g}"]
            if g >= 2:
                wts.append(f"lnTev{g - 2}")
            elif g == 0:
                wts.append("ternev7")
            add("T", lntmm, waits=tuple(wts), key=f"lnTmm{g}")
            add("S", lambda e, t=t, pool=pool: e.activation(
                lnT_st.rearrange("p (c k) -> p c k", c=8)[:, :, t * 128:(t + 1) * 128],
                pool.rearrange("p (c k) -> p c k", c=8), AF.Copy),
                waits=(f"lnTmm{g}",), key=f"lnTev{g}")
        for pi, (wT, bias_c, dst, wkey) in enumerate(
                ((wqT, bq_c, q_sb, "wq"), (wkT, bk_c, k_sb, "wk"))):
            for j in range(8):
                idx = pi * 8 + j
                pool = pqk_a if idx % 2 == 0 else pqk_b
                def qmm(e, wT=wT, j=j, st=st, pool=pool):
                    last = None
                    for c in range(8):
                        last = e.matmul(
                            pool, wT[:, c * D + j * 128: c * D + (j + 1) * 128],
                            lnT_st[:, c * 512:(c + 1) * 512],
                            start=(c == 0), stop=(c == 7))
                    return last
                wts = [f"lnTev{st * 4 + 3}", wkey, "bq", "bk"]
                if idx >= 2:
                    wts.append(f"qkev{st}_{idx - 2}")
                elif st > 0:
                    wts.append(f"qkev{st - 1}_{14 + idx}")
                if st == 0 and idx <= 1:
                    wts.append("ternev7")
                add("T", qmm, waits=tuple(wts), key=f"qkmm{st}_{idx}")
                add("S", lambda e, pool=pool, dst=dst, bias_c=bias_c, j=j, st=st:
                    e.activation(
                        dst[:, j * T_TOK + st * 512: j * T_TOK + (st + 1) * 512],
                        pool, AF.Identity, bias=bias_c[:, j:j + 1]),
                    waits=(f"qkmm{st}_{idx}",), key=f"qkev{st}_{idx}")
        for pi, dst in ((0, q_sb), (1, k_sb)):
            v_in = dst.rearrange("p (j x) -> p j x", j=8)[
                :, :, st * 512:(st + 1) * 512]
            add("V", lambda e, v_in=v_in, pi=pi, st=st: e.scalar_tensor_tensor(
                junk_bf.rearrange("p (j x) -> p j x", j=8)[:, :, 0:512],
                v_in, 1.0, v_in, ALU.mult, ALU.mult,
                accum_out=sqp[:, pi * 4 + st: pi * 4 + st + 1]),
                waits=(f"qkev{st}_{pi * 8 + 7}",), key=f"sq{st}_{pi}",
                fence=True)

    add("V", lambda e: e.tensor_reduce(
        sqf, sqp.rearrange("p (c s) -> p c s", c=2), axis=AX.X, op=ALU.add),
        waits=tuple(f"sq{st}_{pi}" for st in range(NST) for pi in (0, 1)),
        key="sqred", fence=True)
    add("T", lambda e: e.matmul(p_misc[0:1, 0:2], ones128_f, sqf,
                                start=True, stop=True),
        waits=("sqred",), key="sqmm1")
    add("V", lambda e: e.tensor_copy(sq2, p_misc[0:1, 0:2]), waits=("sqmm1",),
        key="sq2")
    add("T", lambda e: e.matmul(p_misc[:, 8:10], ones1_bf, sq2,
                                start=True, stop=True), waits=("sq2",), key="sqmm2")
    add("V", lambda e: e.tensor_copy(sqb, p_misc[:, 8:10]), waits=("sqmm2",),
        key="sqb", fence=True)
    add("S", lambda e: e.activation(tt_c, sqb, AF.Ln), waits=("sqb",))
    add("S", lambda e: e.activation(rq2, tt_c, AF.Exp, scale=-0.5), key="rq")
    add("V", lambda e: e.tensor_tensor(sqn, rq2, rq2, ALU.mult),
        waits=("rq", "sqb"), key="sqnk", fence=True)
    add("V", lambda e: e.tensor_tensor(tt_c, sqn, sqb, ALU.mult),
        waits=("sqnk",), key="ttk", fence=True)
    add("V", lambda e: e.tensor_scalar(lnS0, tt_c, -0.5, -LN8, ALU.mult, ALU.add),
        waits=("ttk",), key="lnS0")

    ternT = wqT
    add("G", lambda e: e.dma_start(
        out=ternT.rearrange("p (c o) -> p c o", c=8),
        in_=dtern.rearrange("(c p) o -> p c o", p=128)),
        waits=tuple(f"terndo{j}" for j in range(8)) + (f"qkmm{NST - 1}_7",),
        dma=True, key="ternld")

    for st in range(NST):
        for c in range(8):
            def phimm(e, c=c, st=st):
                e.matmul(pqk_a, wblk[:, 128 * c:128 * (c + 1)],
                         q_sb[:, c * T_TOK + st * 512: c * T_TOK + (st + 1) * 512],
                         start=True, stop=True)
                return e.matmul(
                    pqk_b, wblk[:, 128 * c:128 * (c + 1)],
                    k_sb[:, c * T_TOK + st * 512: c * T_TOK + (st + 1) * 512],
                    start=True, stop=True)
            wts = ["lnS0", "wblk"]
            if c >= 1:
                wts.append(f"phiev{st}_{c - 1}")
            elif st > 0:
                wts.append(f"phiev{st - 1}_7")
            add("T", phimm, waits=tuple(wts), key=f"phimm{st}_{c}")
            wts = [f"phimm{st}_{c}"]
            if c >= 2:
                wts.append(f"pmul{st}_{c - 2}")
            elif st > 0:
                wts.append(f"pmul{st - 1}_{6 + c}")
            def phiev(e, c=c):
                e.activation(phiq_b[c % 2], pqk_a, AF.Exp,
                             scale=rq2[:, 0:1], bias=lnS0[:, 0:1])
                return e.activation(phik_b[c % 2], pqk_b, AF.Exp,
                                    scale=rq2[:, 1:2], bias=lnS0[:, 1:2])
            add("S", phiev, waits=tuple(wts), key=f"phiev{st}_{c}")
            wts = [f"phiev{st}_{c}"]
            if c >= 2:
                wts.append(f"smm{st}_{c - 2}")
            elif st > 0:
                wts.append(f"smm{st - 1}_{6 + c}")
            add("V", lambda e, c=c: e.tensor_tensor(
                p_b[c % 2], phiq_b[c % 2], phik_b[c % 2], ALU.mult),
                waits=tuple(wts), key=f"pmul{st}_{c}")
            wts = [f"pmul{st}_{c}", "ones_s"]
            if st > 0 and c == 0:
                wts.append(f"ecp{st - 1}")
            add("T", lambda e, c=c: e.matmul(
                p_s, ones_s[:, 16 * c:16 * (c + 1)], p_b[c % 2],
                start=(c == 0), stop=(c == 7)),
                waits=tuple(wts), key=f"smm{st}_{c}")
        add("V", lambda e: e.reciprocal(e_rc, p_s), waits=(f"smm{st}_7",),
            key=f"erc{st}", fence=True)
        add("V", lambda e: e.tensor_tensor(e_fm, p_s, e_rc, ALU.mult),
            waits=(f"erc{st}",), key=f"e{st}")
        def etr(e, st=st):
            last = None
            for t in range(4):
                last = e.matmul(p_misc[:, 64 + 16 * t: 64 + 16 * (t + 1)],
                                e_fm[:, 128 * t:128 * (t + 1)],
                                id16_f32, start=True, stop=True)
            return last
        add("T", etr, waits=(f"e{st}", "idf"), key=f"etr{st}")
        add("V", lambda e: e.tensor_copy(e_tm, p_misc[:, 64:128]),
            waits=(f"etr{st}",), key=f"ecp{st}", fence=True)
        for t in range(4):
            g = st * 4 + t
            xg = x_sb[:, g * D:(g + 1) * D]
            add("S", lambda e, xg=xg, t=t, g=g: e.activation(
                xc_b[t % 2], xg, AF.Identity, bias=negm_s[:, g:g + 1]),
                waits=(f"lnTmmB{g - 2}",) if g >= 2 else ())
            add("S", lambda e, t=t, g=g: e.activation(
                diag_b[t % 2], id_bf, AF.Copy, scale=rstd_s[:, g:g + 1]),
                key=f"xcdB{g}")
            def lntmm(e, t=t):
                last = None
                for c in range(8):
                    last = e.matmul(pool_a[:, 128 * c:128 * (c + 1)],
                                    xc_b[t % 2][:, 128 * c:128 * (c + 1)],
                                    diag_b[t % 2], start=True, stop=True)
                return last
            wts = [f"xcdB{g}"]
            wts.append(f"lnTevB{g - 1}" if g >= 1 else "lnTev15")
            add("T", lntmm, waits=tuple(wts), key=f"lnTmmB{g}")
            add("S", lambda e, t=t: e.activation(
                lnT_b[t % 2], pool_a, AF.Copy),
                waits=(f"lnTmmB{g}",)
                + (tuple(f"terndo{j}" for j in range(8)) if g <= 1 else ()),
                key=f"lnTevB{g}")
            def vmm(e, t=t):
                for c in range(8):
                    e.matmul(pool_b[:, 0:512],
                             lnT_b[t % 2][:, 128 * c:128 * (c + 1)],
                             wvT[:, c * D + 0: c * D + 512],
                             start=(c == 0), stop=False)
                    e.matmul(pool_b[:, 512:1024],
                             lnT_b[t % 2][:, 128 * c:128 * (c + 1)],
                             wvT[:, c * D + 512: c * D + 1024],
                             start=(c == 0), stop=False)
                e.matmul(pool_b[:, 0:512], ones1_bf, bv_bf[:, 0:512],
                         start=False, stop=True)
                return e.matmul(pool_b[:, 512:1024], ones1_bf, bv_bf[:, 512:1024],
                                start=False, stop=True)
            wts = [f"lnTevB{g}", "wv", "bvbf"]
            if g >= 1:
                wts.append(f"x1_{g - 1}")
            add("T", vmm, waits=tuple(wts), key=f"vmm{g}")
            add("V", lambda e, t=t: e.tensor_tensor(
                d_sc.rearrange("p (h w) -> p h w", h=16),
                pool_b.rearrange("p (h w) -> p h w", h=16),
                e_tm[:, 16 * t:16 * (t + 1)].unsqueeze(2).broadcast_to(
                    [128, 16, 64]),
                ALU.mult), waits=(f"vmm{g}", f"ecp{st}"), key=f"am{g}",
                fence=True)
            add("V", lambda e, xg=xg: e.tensor_tensor(xg, xg, d_sc, ALU.add),
                waits=(f"am{g}",), key=f"x1_{g}", fence=True)
    add("Q", lambda e: e.dma_start(out=had_sb, in_=dhad),
        waits=(f"phimm{NST - 1}_7",), dma=True, key="had")

    qT2_b = [lnT_st[:, 0:1024], lnT_st[:, 1024:2048]]
    qbf2_b = [lnT_st[:, 2048:3072], lnT_st[:, 3072:4096]]

    def stage1a(g):
        xg = x_sb[:, g * D:(g + 1) * D]
        bn2 = bn2_b[g % 2]
        add("V", lambda e, xg=xg, bn2=bn2, g=g: e.scalar_tensor_tensor(
            cjunk, xg, 1.0, xg, ALU.mult, ALU.max, accum_out=bn2[:, 0:1]),
            waits=(f"x1_{g}",)
            + ((f"xcdC{g - 2}",) if g >= 2 else (f"phimm{NST - 1}_7",)),
            key=f"sumC{g}", fence=True)
        add("V", lambda e, xg=xg, bn2=bn2: e.scalar_tensor_tensor(
            cjunk, xg, 1.0, xg, ALU.mult, ALU.mult, accum_out=bn2[:, 1:2]),
            key=f"ssC{g}", fence=True)
        add("V", lambda e, g=g, bn2=bn2: e.tensor_scalar_mul(
            negm_s[:, g:g + 1], bn2[:, 0:1], -1.0 / D), waits=(f"sumC{g}",),
            key=f"nmC{g}", fence=True)
        add("V", lambda e, g=g, bn2=bn2: e.tensor_tensor(
            bn6[:, 0:1], bn2[:, 0:1], negm_s[:, g:g + 1], ALU.mult),
            waits=(f"nmC{g}",), key=f"t2C{g}", fence=True)
        add("V", lambda e, bn2=bn2: e.tensor_tensor(
            bn2[:, 1:2], bn2[:, 1:2], bn6[:, 0:1], ALU.add),
            waits=(f"ssC{g}", f"t2C{g}"), key=f"stC{g}")
        add("S", lambda e, bn2=bn2: e.activation(
            stat1, bn2[:, 1:2], AF.Ln, bias=eps_c, scale=1.0 / D),
            waits=(f"stC{g}",))
        add("S", lambda e, g=g: e.activation(
            rstd_s[:, g:g + 1], stat1, AF.Exp, scale=-0.5))
        add("S", lambda e, xg=xg, g=g: e.activation(
            xc_b[g % 2], xg, AF.Identity, bias=negm_s[:, g:g + 1]),
            waits=(f"ln2mm{g - 2}",) if g >= 2 else ())
        add("S", lambda e, g=g: e.activation(
            diag_b[g % 2], id_bf, AF.Copy, scale=rstd_s[:, g:g + 1]),
            key=f"xcdC{g}")

        def ln2mm(e, g=g):
            last = None
            for c in range(8):
                last = e.matmul(pool_a[:, 128 * c:128 * (c + 1)],
                                xc_b[g % 2][:, 128 * c:128 * (c + 1)],
                                diag_b[g % 2], start=True, stop=True)
            return last
        wts = [f"xcdC{g}"]
        if g == 0:
            wts.append("lnTevB15")
        elif g <= 2:
            wts.append(f"ln2ev{g - 1}")
        else:
            wts.append(f"qTev{g - 3}")
        add("T", ln2mm, waits=tuple(wts), key=f"ln2mm{g}")
        add("S", lambda e, g=g: e.activation(lnT_b[g % 2], pool_a, AF.Copy),
            waits=(f"ln2mm{g}",) + ((f"xhmm{g - 2}",) if g >= 2 else ()),
            key=f"ln2ev{g}")

    def stage1b(g):
        def xhmm(e, g=g):
            last = None
            for c in range(8):
                e.matmul(pool_b[:, 0:512],
                         lnT_b[g % 2][:, 128 * c:128 * (c + 1)],
                         had_sb[:, c * D + 0: c * D + 512],
                         start=(c == 0), stop=(c == 7))
                last = e.matmul(pool_b[:, 512:1024],
                                lnT_b[g % 2][:, 128 * c:128 * (c + 1)],
                                had_sb[:, c * D + 512: c * D + 1024],
                                start=(c == 0), stop=(c == 7))
            return last
        wts = [f"ln2ev{g}", "had"]
        if g >= 1:
            wts += [f"dq{g - 1}", f"rm{g - 1}"]
        else:
            wts.append("x1_15")
        add("T", xhmm, waits=tuple(wts), key=f"xhmm{g}")

    def stage2a(g):
        add("V", lambda e: e.tensor_reduce(
            rm_c, pool_b, axis=AX.X, op=ALU.max, apply_absolute_value=True),
            waits=(f"xhmm{g}",), key=f"rm{g}", fence=True)
        add("V", lambda e: e.tensor_scalar_max(cc_c, rm_c, 1e-5),
            waits=(f"rm{g}",), key=f"cc{g}", fence=True)
        add("V", lambda e: e.reciprocal(rm_c, cc_c),
            waits=(f"cc{g}",), key=f"rcc{g}", fence=True)
        add("V", lambda e, g=g: e.tensor_scalar_mul(
            rs_b[g % 2], rm_c, 7.0),
            waits=(f"rcc{g}",) + ((f"dq{g - 2}",) if g >= 2 else ()),
            key=f"rs{g}", fence=True)
        add("V", lambda e, g=g: e.tensor_scalar_mul(
            scale_b[g % 2], cc_c, -1.0 / 7.0),
            waits=(f"cc{g}",), key=f"qsc{g}", fence=True)
        add("S", lambda e, g=g: e.activation(
            d_b[g % 2], pool_b, AF.Copy, scale=rs_b[g % 2]),
            waits=(f"rs{g}",) + ((f"qc{g - 2}",) if g >= 2 else ()),
            key=f"dq{g}")

    def stage2b(g):
        xg = x_sb[:, g * D:(g + 1) * D]
        add("S", lambda e, g=g: e.copy(qi8_b[g % 2], d_b[g % 2]))
        add("S", lambda e, g=g: e.copy(qbf2_b[g % 2], qi8_b[g % 2]),
            waits=(f"qtmm{g - 2}",) if g >= 2 else (), key=f"qb{g}")
        add("V", lambda e, g=g: e.tensor_tensor(
            sg_sc, d_b[g % 2], qbf2_b[g % 2], ALU.subtract),
            waits=(f"qb{g}",), key=f"qc{g}", fence=True)
        add("V", lambda e, g=g: e.scalar_tensor_tensor(
            qi8_b[g % 2], sg_sc, 0.5, qbf2_b[g % 2], ALU.is_gt, ALU.add),
            waits=(f"qc{g}",), key=f"qe{g}", fence=True)
        add("V", lambda e, g=g: e.scalar_tensor_tensor(
            sg_bf, sg_sc, -0.5, qi8_b[g % 2], ALU.is_lt, ALU.subtract),
            waits=(f"qe{g}",), key=f"qf{g}", fence=True)
        add("V", lambda e, g=g: e.tensor_scalar(
            qbf2_b[g % 2], sg_bf, -7.0, 8.0, ALU.max, ALU.min),
            waits=(f"qf{g}",), key=f"qbf{g}")

        def qtmm(e, g=g):
            last = None
            for c in range(8):
                last = e.matmul(pool_a[:, 128 * c:128 * (c + 1)],
                                qbf2_b[g % 2][:, 128 * c:128 * (c + 1)],
                                id_bf, start=True, stop=True)
            return last
        add("T", qtmm,
            waits=(f"qbf{g}",
                   f"ln2ev{g + 2}" if g < NT - 2 else f"qTev{g - 1}"),
            key=f"qtmm{g}")
        add("S", lambda e, g=g: e.activation(qT2_b[g % 2], pool_a, AF.Copy),
            waits=(f"qtmm{g}",) + ((f"omm{g - 2}",) if g >= 2 else ()),
            key=f"qTev{g}")

        def omm(e, g=g):
            last = None
            for c in range(8):
                e.matmul(pqk_a,
                         qT2_b[g % 2][:, 128 * c:128 * (c + 1)],
                         ternT[:, c * D + 0: c * D + 512],
                         start=(c == 0), stop=(c == 7))
                last = e.matmul(pqk_b,
                                qT2_b[g % 2][:, 128 * c:128 * (c + 1)],
                                ternT[:, c * D + 512: c * D + 1024],
                                start=(c == 0), stop=(c == 7))
            return last
        wts = [f"qTev{g}", f"dq{g}", f"qsc{g}", "ternld"]
        if g >= 1:
            wts.append(f"x2_{g - 1}")
        else:
            wts.append("phiev3_7")
        add("T", omm, waits=tuple(wts), key=f"omm{g}")
        add("V", lambda e, xg=xg, g=g: e.scalar_tensor_tensor(
            xg[:, 0:512], pqk_a, scale_b[g % 2], xg[:, 0:512],
            ALU.mult, ALU.add),
            waits=(f"omm{g}", f"qsc{g}"), key=f"x2a_{g}")
        add("V", lambda e, xg=xg, g=g: e.scalar_tensor_tensor(
            xg[:, 512:1024], pqk_b, scale_b[g % 2], xg[:, 512:1024],
            ALU.mult, ALU.add), key=f"x2_{g}")

    stage1a(0)
    stage1a(1)
    stage1b(0)
    stage2a(0)
    stage1a(2)
    stage1b(1)
    for g in range(NT):
        if g + 1 < NT:
            stage2a(g + 1)
        if g + 2 < NT:
            stage1b(g + 2)
        stage2b(g)
        if g + 3 < NT:
            stage1a(g + 3)
        if g % 4 == 3:
            st = g // 4
            add("Q", lambda e, st=st: e.dma_start(
                out=dout[:, st * 4 * D:(st + 1) * 4 * D],
                in_=x_sb[:, st * 4 * D:(st + 1) * 4 * D]),
                waits=tuple(f"x2a_{st * 4 + t}" for t in range(4))
                + tuple(f"x2_{st * 4 + t}" for t in range(4)),
                dma=True, key=f"store{st}")

    def final_wait(e):
        for st in range(NST):
            e.wait_ge(sch.dma_sem(f"store{st}"), 16)
        return None
    add("Q", final_wait)

    with nc.Block() as block:
        @block.tensor
        def _(t):
            sch.emit_engine("T", t)

        @block.vector
        def _(v):
            sch.emit_engine("V", v)

        @block.scalar
        def _(s):
            sch.emit_engine("S", s)

        @block.sync
        def _(q):
            sch.emit_engine("Q", q)

        @block.gpsimd
        def _(g):
            sch.emit_engine("G", g)
    return nc


_HAD = None
def _hadamard_bf16():
    global _HAD
    if _HAD is None:
        h = np.array([[1.0]], dtype=np.float64)
        while h.shape[0] < D:
            h = np.block([[h, h], [h, -h]])
        _HAD = (h / np.sqrt(D)).astype(np.float32).astype(ml_dtypes.bfloat16)
    return _HAD


def _sb_layout(w):
    n = w.shape[1]
    return np.ascontiguousarray(
        w.reshape(8, 128, n).transpose(1, 0, 2).reshape(128, 8 * n))


def make_inputs_common(Wq, bq, Wk, bk, Wv, bv, weights, wffn):
    wgt = np.asarray(weights, np.float32)
    wblk = np.zeros((8, 128, 128), np.float32)
    for c in range(8):
        for hh in range(2):
            wblk[c, 64 * hh:64 * (hh + 1), 64 * hh:64 * (hh + 1)] = wgt[2 * c + hh].T
    ones_s = np.zeros((128, 128), np.float32)
    for c in range(8):
        for hh in range(2):
            h = 2 * c + hh
            ones_s[64 * hh:64 * (hh + 1), 16 * c + h] = 1.0
    idm = np.eye(128, dtype=np.float32)
    return {
        "wqT": _sb_layout(np.ascontiguousarray(np.asarray(Wq, np.float32).T)),
        "wkT": _sb_layout(np.ascontiguousarray(np.asarray(Wk, np.float32).T)),
        "wvT": _sb_layout(np.ascontiguousarray(np.asarray(Wv, np.float32).T)),
        "bq": np.ascontiguousarray(np.asarray(bq, np.float32).reshape(8, 128).T),
        "bk": np.ascontiguousarray(np.asarray(bk, np.float32).reshape(8, 128).T),
        "bv": np.ascontiguousarray(np.asarray(bv, np.float32)),
        "wblk": np.ascontiguousarray(
            wblk.transpose(1, 0, 2).reshape(128, 8 * 128)),
        "wffn": np.ascontiguousarray(
            np.asarray(wffn, np.float32).reshape(8, 128, D)),
        "had": _sb_layout(_hadamard_bf16()),
        "ones_s": ones_s.astype(ml_dtypes.bfloat16),
        "id_bf": idm.astype(ml_dtypes.bfloat16),
        "id16_f32": np.eye(16, dtype=np.float32),
    }


_NC = None
def kernel(L0_attn_x, L0_attn_Wq, L0_attn_bq, L0_attn_Wk, L0_attn_bk,
           L0_attn_Wv, L0_attn_bv, L0_attn_weights, L0_ffn_weight):
    global _NC
    if _NC is None:
        _NC = build_program()
    nc = _NC
    x = np.asarray(L0_attn_x, dtype=np.float32)
    xl = np.ascontiguousarray(
        x.reshape(NCORES, NT, 128, D).transpose(0, 2, 1, 3).reshape(
            NCORES, 128, NT * D))
    common = make_inputs_common(L0_attn_Wq, L0_attn_bq, L0_attn_Wk, L0_attn_bk,
                                L0_attn_Wv, L0_attn_bv, L0_attn_weights,
                                L0_ffn_weight)
    in_maps = [dict(common, x=xl[b]) for b in range(NCORES)]
    res = run_bass_kernel_spmd(nc, in_maps, core_ids=list(range(NCORES)))
    out = np.stack([np.asarray(res.results[b]["out"]) for b in range(NCORES)],
                   axis=0)
    out = out.reshape(NCORES, 128, NT, D).transpose(0, 2, 1, 3).reshape(
        NCORES, T_TOK, D)
    return np.ascontiguousarray(out).astype(np.float32)


# version 16
# speedup vs baseline: 1.0392x; 1.0392x over previous
import numpy as np
import ml_dtypes
import concourse.bass as bass
import concourse.mybir as mybir
from concourse.bass_utils import run_bass_kernel_spmd

F32 = mybir.dt.float32
BF16 = mybir.dt.bfloat16
I8 = mybir.dt.int8
AF = mybir.ActivationFunctionType
ALU = mybir.AluOpType
AX = mybir.AxisListType

NCORES = 8
T_TOK = 2048
D = 1024
NT = 16
NST = 4
LN8 = 2.0794415416798357


class Sched:

    CENGS = ("T", "V", "S")
    ENGS = ("T", "V", "S", "Q", "G")

    def __init__(self, nc):
        self.nc = nc
        self.sems = {e: nc.alloc_semaphore(f"sem{e}") for e in self.CENGS}
        self.count = {e: 0 for e in self.CENGS}
        self.keys = {}
        self.ops = {e: [] for e in self.ENGS}
        self.ndma = 0

    def add(self, eng, fn, waits=(), key=None, dma=False, fence=False):
        for w in waits:
            assert w in self.keys, f"wait on unknown key {w}"
        if key is not None:
            assert key not in self.keys, f"dup key {key}"
        if dma:
            assert key is not None, "every DMA needs a key/semaphore"
            sem = self.nc.alloc_semaphore(f"dma{self.ndma}")
            self.ndma += 1
            self.keys[key] = ("dma", sem)
            self.ops[eng].append((fn, tuple(self.keys[w] for w in waits), sem, 0))
        else:
            inc = 1 if key is not None else 0
            assert not (fence and not inc), "fence requires a key"
            if inc:
                self.count[eng] += inc
                self.keys[key] = ("eng", eng, self.count[eng], fence)
            self.ops[eng].append((fn, tuple(self.keys[w] for w in waits), None, inc))

    def dma_sem(self, key):
        kind, sem = self.keys[key][0], self.keys[key][1]
        assert kind == "dma"
        return sem

    def emit_engine(self, e, eng):
        hw = {x: 0 for x in self.CENGS}
        dma_seen = set()
        for fn, waits, dsem, inc in self.ops[e]:
            for w in waits:
                if w[0] == "eng":
                    _, we, wv, wfence = w
                    if we == e and not wfence:
                        continue
                    if wv > hw[we]:
                        eng.wait_ge(self.sems[we], wv)
                        hw[we] = wv
                else:
                    sem = w[1]
                    if id(sem) not in dma_seen:
                        eng.wait_ge(sem, 16)
                        dma_seen.add(id(sem))
            ins = fn(eng)
            if dsem is not None:
                assert ins is not None, "dma op returned no instr"
                ins.then_inc(dsem, 16)
            elif inc:
                assert ins is not None, "op with inc returned no instr"
                ins.then_inc(self.sems[e], inc)


def build_program():
    nc = bass.Bass("TRN2", target_bir_lowering=False, debug=False,
                   detect_race_conditions=False,
                   dynamic_dma_scratch_size=16384)

    dx = nc.dram_tensor("x", [128, NT * D], F32, kind="ExternalInput").ap()
    dwqT = nc.dram_tensor("wqT", [128, 8 * D], F32, kind="ExternalInput").ap()
    dwkT = nc.dram_tensor("wkT", [128, 8 * D], F32, kind="ExternalInput").ap()
    dwvT = nc.dram_tensor("wvT", [128, 8 * D], F32, kind="ExternalInput").ap()
    dbq = nc.dram_tensor("bq", [128, 8], F32, kind="ExternalInput").ap()
    dbk = nc.dram_tensor("bk", [128, 8], F32, kind="ExternalInput").ap()
    dbv = nc.dram_tensor("bv", [D], F32, kind="ExternalInput").ap()
    dwblk = nc.dram_tensor("wblk", [128, 8 * 128], F32, kind="ExternalInput").ap()
    dwffn = nc.dram_tensor("wffn", [8, 128, D], F32, kind="ExternalInput").ap()
    dhad = nc.dram_tensor("had", [128, 8 * D], BF16, kind="ExternalInput").ap()
    dones_s = nc.dram_tensor("ones_s", [128, 128], BF16, kind="ExternalInput").ap()
    did_bf = nc.dram_tensor("id_bf", [128, 128], BF16, kind="ExternalInput").ap()
    did16 = nc.dram_tensor("id16_f32", [16, 16], F32, kind="ExternalInput").ap()
    dout = nc.dram_tensor("out", [128, NT * D], F32, kind="ExternalOutput").ap()
    dtern = nc.dram_tensor("dtern", [D, D], BF16, kind="Internal").ap()

    sb = lambda n, s, d=F32: nc.alloc_sbuf_tensor(n, s, d).ap()
    x_sb = sb("x_sb", [128, NT * D])
    wqT = sb("wqT_sb", [128, 8 * D], BF16)
    wkT = sb("wkT_sb", [128, 8 * D], BF16)
    wvT = sb("wvT_sb", [128, 8 * D], BF16)
    q_sb = sb("q_sb", [128, 8 * T_TOK], BF16)
    k_sb = sb("k_sb", [128, 8 * T_TOK], BF16)
    had_sb = k_sb[:, 0:8 * D]
    wblk = sb("wblk_sb", [128, 8 * 128], BF16)
    ones_s = sb("ones_s_sb", [128, 128], BF16)
    id_bf = sb("id_bf_sb", [128, 128], BF16)
    id16_f32 = sb("id16_sb", [16, 16], F32)
    bq_c = sb("bq_c", [128, 8])
    bk_c = sb("bk_c", [128, 8])
    bv_bf = sb("bv_bf", [1, D], BF16)
    ones1_bf = sb("ones1_bf", [1, 128], BF16)
    ones128_f = sb("ones128_f", [128, 1], F32)
    eps_c = sb("eps_c", [128, 1])
    negm_s = sb("negm_s", [128, NT])
    rstd_s = sb("rstd_s", [128, NT])
    bn6 = sb("bn6", [128, 12])
    bn2_b = [sb(f"bn2_{i}", [128, 2]) for i in range(2)]
    stat1 = sb("stat1", [128, 1])
    lnT_st = sb("lnT_st", [128, 8 * 512], BF16)
    xc_b = [sb(f"xc{i}", [128, D], BF16) for i in range(2)]
    diag_b = [sb(f"diag{i}", [128, 128], BF16) for i in range(2)]
    lnT_b = [sb(f"lnT{i}", [128, 8 * 128], BF16) for i in range(2)]
    qT_b = lnT_b
    qbf_b = xc_b
    qi8_1 = sb("qi8", [128, D], I8)
    qi8_b = [qi8_1, qi8_1]
    phiq_b = [lnT_st[:, 512 * i:512 * (i + 1)] for i in range(2)]
    phik_b = [lnT_st[:, 1024 + 512 * i:1024 + 512 * (i + 1)] for i in range(2)]
    p_b = [lnT_st[:, 2048 + 512 * i:2048 + 512 * (i + 1)] for i in range(2)]
    rm_c = sb("rm_c", [128, 1])
    cc_c = sb("cc_c", [128, 1])
    rs_b = [sb(f"rs{i}", [128, 1]) for i in range(2)]
    scale_b = [sb(f"scale{i}", [128, 1]) for i in range(2)]
    sqp = sb("sqp", [128, 8])
    sqf = sb("sqf", [128, 2])
    sq2 = sb("sq2", [1, 2], BF16)
    sqb = sb("sqb", [128, 2])
    rq2 = sb("rq2", [128, 2])
    sqn = sb("sqn", [128, 2])
    lnS0 = sb("lnS0", [128, 2])
    tt_c = sb("tt_c", [128, 2])
    swcol = sb("swcol", [128, 8])
    swsc = sb("swsc", [128, 8])
    swrcp = sb("swrcp", [128, 8])
    arena = sb("arena", [128, 8192], mybir.dt.uint8)
    junk_bf = arena.bitcast(BF16)
    sg_sc = arena.bitcast(F32)[:, 1024:2048]
    e_fm = arena.bitcast(F32)[0:16, 0:512]
    e_tm = arena.bitcast(F32)[:, 1024:1088]
    e_rc = arena.bitcast(F32)[0:16, 512:1024]
    d_sc = arena.bitcast(F32)[:, 0:1024]
    sg_bf = arena.bitcast(BF16)[:, 2048:3072]
    sg_bf2 = arena.bitcast(BF16)[:, 3072:4096]
    d2_sc = q_sb.bitcast(F32)[:, 0:1024]
    cjunk = q_sb.bitcast(F32)[:, 1024:2048]
    d_b = [d_sc, d2_sc]
    qf32 = q_sb.bitcast(F32)
    wst_b = [qf32[:, 1024 * i:1024 * (i + 1)] for i in range(2)]
    nwork_b = [qf32[:, 2048 + 1024 * i:2048 + 1024 * (i + 1)] for i in range(2)]
    tern_b = [q_sb[:, 8192 + 1024 * i:8192 + 1024 * (i + 1)] for i in range(2)]

    ps = lambda n, s: nc.alloc_psum_tensor(n, s, F32).ap()
    pool_a = ps("pool_a", [128, 1024])
    pool_b = ps("pool_b", [128, 1024])
    pqk_a = ps("pqk_a", [128, 512])
    pqk_b = ps("pqk_b", [128, 512])
    p_s = ps("p_s", [16, 512])
    p_misc = ps("p_misc", [128, 512])

    sch = Sched(nc)
    add = sch.add

    for st in range(NST):
        add("Q", lambda e, st=st: e.dma_start(
            out=x_sb[:, st * 4 * D:(st + 1) * 4 * D],
            in_=dx[:, st * 4 * D:(st + 1) * 4 * D]),
            dma=True, key=f"x{st}")
    add("Q", lambda e: e.dma_start(out=id_bf, in_=did_bf), dma=True, key="idb")
    add("Q", lambda e: e.dma_start(out=id16_f32, in_=did16), dma=True, key="idf")
    add("Q", lambda e: e.dma_start(out=ones_s, in_=dones_s), dma=True, key="ones_s")
    add("Q", lambda e: e.dma_start(out=bq_c, in_=dbq), dma=True, key="bq")
    add("Q", lambda e: e.dma_start(out=bk_c, in_=dbk), dma=True, key="bk")

    def add_wffn(j, waits=()):
        add("Q", lambda e, j=j: e.dma_start(
            out=wst_b[j % 2], in_=dwffn[j]),
            dma=True, key=f"wffn{j}", waits=waits)
    add_wffn(0)
    add_wffn(1)
    for nm, ddst, dsrc in (("wq", wqT, dwqT), ("wk", wkT, dwkT), ("wv", wvT, dwvT)):
        add("G", lambda e, ddst=ddst, dsrc=dsrc: e.dma_start(
            out=ddst, in_=dsrc), dma=True, key=nm)
    add("G", lambda e: e.dma_start(out=wblk, in_=dwblk), dma=True, key="wblk")
    add("G", lambda e: e.dma_start(out=bv_bf, in_=dbv[None, :]), dma=True,
        key="bvbf")

    add("V", lambda e: e.memset(ones1_bf, 1.0))
    add("V", lambda e: e.memset(ones128_f, 1.0))
    add("V", lambda e: e.memset(eps_c, 1e-5), key="konst")

    for j in range(8):
        w = wst_b[j % 2]
        nwk = nwork_b[j % 2]
        add("V", lambda e, w=w, j=j: e.tensor_reduce(
            swcol[:, j:j + 1], w, axis=AX.X, op=ALU.add,
            apply_absolute_value=True), waits=(f"wffn{j}",),
            key=f"swred{j}", fence=True)
        add("V", lambda e, j=j: e.tensor_scalar(
            swsc[:, j:j + 1], swcol[:, j:j + 1], 1.0 / 1024.0, 1e-5,
            ALU.mult, ALU.max), waits=(f"swred{j}",), key=f"swsc{j}",
            fence=True)
        add("V", lambda e, j=j: e.reciprocal(swrcp[:, j:j + 1], swsc[:, j:j + 1]),
            waits=(f"swsc{j}",), key=f"swr{j}")
        add("S", lambda e, w=w, nwk=nwk, j=j: e.activation(
            nwk, w, AF.Copy, scale=swrcp[:, j:j + 1]),
            waits=(f"swr{j}",), key=f"nq{j}")
        add("V", lambda e, nwk=nwk, j=j: e.tensor_scalar(
            sg_sc, nwk, -0.5, None, ALU.is_lt), waits=(f"nq{j}",),
            key=f"sgp{j}", fence=True)
        add("V", lambda e, nwk=nwk, j=j: e.scalar_tensor_tensor(
            tern_b[j % 2], nwk, 0.5, sg_sc, ALU.is_gt, ALU.subtract),
            waits=(f"sgp{j}",), key=f"ternq{j}", fence=True)
        add("V", lambda e, j=j: e.tensor_scalar(
            tern_b[j % 2], tern_b[j % 2], swsc[:, j:j + 1], None, ALU.mult),
            waits=(f"ternq{j}",), key=f"tern{j}")
        def trn(e, j=j):
            last = None
            for c in range(8):
                last = e.matmul(pool_a[:, 128 * c:128 * (c + 1)],
                                tern_b[j % 2][:, 128 * c:128 * (c + 1)],
                                id_bf, start=True, stop=True)
            return last
        add("T", trn,
            waits=(f"tern{j}", "idb") + ((f"ternev{j - 1}",) if j else ()),
            key=f"ternmm{j}")
        add("S", lambda e, j=j: e.activation(
            lnT_b[j % 2], pool_a, AF.Copy),
            waits=(f"ternmm{j}",) + ((f"terndo{j - 2}",) if j >= 2 else ()),
            key=f"ternev{j}")
        add("G", lambda e, j=j: e.dma_start(
            out=dtern.rearrange("(c p) o -> p c o", p=128)[:, :, 128 * j:128 * (j + 1)],
            in_=lnT_b[j % 2].rearrange("p (c m) -> p c m", c=8)),
            waits=(f"ternev{j}",), dma=True, key=f"terndo{j}")
        add("S", lambda e: e.nop(), waits=(f"nq{j}",), key=f"ternrd{j}")
        if j + 2 < 8:
            add_wffn(j + 2, waits=(f"ternrd{j}",))

    def a_ln_tile(g, emit_mm):
        st, t = g // 4, g % 4
        xg = x_sb[:, g * D:(g + 1) * D]
        bn2 = bn2_b[t % 2]
        if not emit_mm:
            add("V", lambda e, xg=xg, bn2=bn2, g=g: e.scalar_tensor_tensor(
                d_sc, xg, 1.0, xg, ALU.mult, ALU.max, accum_out=bn2[:, 0:1]),
                waits=(f"x{st}",) + ((f"xcd{g - 2}",) if g >= 2 else ()),
                key=f"sum{g}", fence=True)
            add("V", lambda e, xg=xg, bn2=bn2: e.scalar_tensor_tensor(
                d_sc, xg, 1.0, xg, ALU.mult, ALU.mult, accum_out=bn2[:, 1:2]),
                key=f"ss{g}", fence=True)
            add("V", lambda e, g=g, bn2=bn2: e.tensor_scalar_mul(
                negm_s[:, g:g + 1], bn2[:, 0:1], -1.0 / D), waits=(f"sum{g}",),
                key=f"nm{g}", fence=True)
            add("V", lambda e, g=g, bn2=bn2: e.tensor_tensor(
                bn6[:, 0:1], bn2[:, 0:1], negm_s[:, g:g + 1], ALU.mult),
                waits=(f"nm{g}",), key=f"t2{g}", fence=True)
            add("V", lambda e, bn2=bn2: e.tensor_tensor(
                bn2[:, 1:2], bn2[:, 1:2], bn6[:, 0:1], ALU.add),
                waits=(f"ss{g}", f"t2{g}"), key=f"st{g}")
            add("S", lambda e, bn2=bn2: e.activation(
                stat1, bn2[:, 1:2], AF.Ln, bias=eps_c, scale=1.0 / D),
                waits=(f"st{g}", "konst"))
            add("S", lambda e, g=g: e.activation(
                rstd_s[:, g:g + 1], stat1, AF.Exp, scale=-0.5))
            add("S", lambda e, xg=xg, t=t, g=g: e.activation(
                xc_b[t % 2], xg, AF.Identity, bias=negm_s[:, g:g + 1]))
            add("S", lambda e, t=t, g=g: e.activation(
                diag_b[t % 2], id_bf, AF.Copy, scale=rstd_s[:, g:g + 1]),
                waits=("idb",), key=f"xcd{g}")
            return
        pool = pool_a if t % 2 == 0 else pool_b

        def lntmm(e, t=t, pool=pool):
            last = None
            for c in range(8):
                last = e.matmul(pool[:, 128 * c:128 * (c + 1)],
                                xc_b[t % 2][:, 128 * c:128 * (c + 1)],
                                diag_b[t % 2], start=True, stop=True)
            return last
        wts = [f"xcd{g}"]
        if g >= 2:
            wts.append(f"lnTev{g - 2}")
        elif g == 0:
            wts.append("ternev7")
        add("T", lntmm, waits=tuple(wts), key=f"lnTmm{g}")

    def a_ln_ev(g):
        st, t = g // 4, g % 4
        pool = pool_a if t % 2 == 0 else pool_b
        evw = [f"lnTmm{g}"]
        if t == 0 and st >= 1:
            evw.append(f"qkmm{st - 1}_15")
        add("S", lambda e, t=t, pool=pool: e.activation(
            lnT_st.rearrange("p (c k) -> p c k", c=8)[:, :, t * 128:(t + 1) * 128],
            pool.rearrange("p (c k) -> p c k", c=8), AF.Copy),
            waits=tuple(evw), key=f"lnTev{g}")

    def a_qk(st):
        for pi, (wT, bias_c, dst, wkey) in enumerate(
                ((wqT, bq_c, q_sb, "wq"), (wkT, bk_c, k_sb, "wk"))):
            for j in range(8):
                idx = pi * 8 + j
                pool = pqk_a if idx % 2 == 0 else pqk_b

                def qmm(e, wT=wT, j=j, st=st, pool=pool):
                    last = None
                    for c in range(8):
                        last = e.matmul(
                            pool, wT[:, c * D + j * 128: c * D + (j + 1) * 128],
                            lnT_st[:, c * 512:(c + 1) * 512],
                            start=(c == 0), stop=(c == 7))
                    return last
                wts = [f"lnTev{st * 4 + 3}", wkey, "bq", "bk"]
                if idx >= 2:
                    wts.append(f"qkev{st}_{idx - 2}")
                elif st > 0:
                    wts.append(f"qkev{st - 1}_{14 + idx}")
                if st == 0 and idx <= 1:
                    wts.append("ternev7")
                add("T", qmm, waits=tuple(wts), key=f"qkmm{st}_{idx}")
                add("S", lambda e, pool=pool, dst=dst, bias_c=bias_c, j=j, st=st:
                    e.activation(
                        dst[:, j * T_TOK + st * 512: j * T_TOK + (st + 1) * 512],
                        pool, AF.Identity, bias=bias_c[:, j:j + 1]),
                    waits=(f"qkmm{st}_{idx}",), key=f"qkev{st}_{idx}")
        for pi, dst in ((0, q_sb), (1, k_sb)):
            v_in = dst.rearrange("p (j x) -> p j x", j=8)[
                :, :, st * 512:(st + 1) * 512]
            add("V", lambda e, v_in=v_in, pi=pi, st=st: e.scalar_tensor_tensor(
                junk_bf.rearrange("p (j x) -> p j x", j=8)[:, :, 0:512],
                v_in, 1.0, v_in, ALU.mult, ALU.mult,
                accum_out=sqp[:, pi * 4 + st: pi * 4 + st + 1]),
                waits=(f"qkev{st}_{pi * 8 + 7}",), key=f"sq{st}_{pi}",
                fence=True)

    for t in range(4):
        a_ln_tile(t, False)
        a_ln_tile(t, True)
        a_ln_ev(t)
    for st in range(NST):
        if st + 1 < NST:
            for t in range(4):
                a_ln_tile((st + 1) * 4 + t, False)
            a_ln_tile((st + 1) * 4 + 0, True)
            a_ln_tile((st + 1) * 4 + 1, True)
        a_qk(st)
        if st + 1 < NST:
            a_ln_ev((st + 1) * 4 + 0)
            a_ln_ev((st + 1) * 4 + 1)
            a_ln_tile((st + 1) * 4 + 2, True)
            a_ln_ev((st + 1) * 4 + 2)
            a_ln_tile((st + 1) * 4 + 3, True)
            a_ln_ev((st + 1) * 4 + 3)

    add("V", lambda e: e.tensor_reduce(
        sqf, sqp.rearrange("p (c s) -> p c s", c=2), axis=AX.X, op=ALU.add),
        waits=tuple(f"sq{st}_{pi}" for st in range(NST) for pi in (0, 1)),
        key="sqred", fence=True)
    add("T", lambda e: e.matmul(p_misc[0:1, 0:2], ones128_f, sqf,
                                start=True, stop=True),
        waits=("sqred",), key="sqmm1")
    add("V", lambda e: e.tensor_copy(sq2, p_misc[0:1, 0:2]), waits=("sqmm1",),
        key="sq2")
    add("T", lambda e: e.matmul(p_misc[:, 8:10], ones1_bf, sq2,
                                start=True, stop=True), waits=("sq2",), key="sqmm2")
    add("V", lambda e: e.tensor_copy(sqb, p_misc[:, 8:10]), waits=("sqmm2",),
        key="sqb", fence=True)
    add("S", lambda e: e.activation(tt_c, sqb, AF.Ln), waits=("sqb",))
    add("S", lambda e: e.activation(rq2, tt_c, AF.Exp, scale=-0.5), key="rq")
    add("V", lambda e: e.tensor_tensor(sqn, rq2, rq2, ALU.mult),
        waits=("rq", "sqb"), key="sqnk", fence=True)
    add("V", lambda e: e.tensor_tensor(tt_c, sqn, sqb, ALU.mult),
        waits=("sqnk",), key="ttk", fence=True)
    add("V", lambda e: e.tensor_scalar(lnS0, tt_c, -0.5, -LN8, ALU.mult, ALU.add),
        waits=("ttk",), key="lnS0")

    ternT = wqT
    add("G", lambda e: e.dma_start(
        out=ternT.rearrange("p (c o) -> p c o", c=8),
        in_=dtern.rearrange("(c p) o -> p c o", p=128)),
        waits=tuple(f"terndo{j}" for j in range(8)) + (f"qkmm{NST - 1}_7",),
        dma=True, key="ternld")

    for st in range(NST):
        for c in range(8):
            def phimm(e, c=c, st=st):
                e.matmul(pqk_a, wblk[:, 128 * c:128 * (c + 1)],
                         q_sb[:, c * T_TOK + st * 512: c * T_TOK + (st + 1) * 512],
                         start=True, stop=True)
                return e.matmul(
                    pqk_b, wblk[:, 128 * c:128 * (c + 1)],
                    k_sb[:, c * T_TOK + st * 512: c * T_TOK + (st + 1) * 512],
                    start=True, stop=True)
            wts = ["lnS0", "wblk"]
            if c >= 1:
                wts.append(f"phiev{st}_{c - 1}")
            elif st > 0:
                wts.append(f"phiev{st - 1}_7")
            add("T", phimm, waits=tuple(wts), key=f"phimm{st}_{c}")
            wts = [f"phimm{st}_{c}"]
            if c >= 2:
                wts.append(f"pmul{st}_{c - 2}")
            elif st > 0:
                wts.append(f"pmul{st - 1}_{6 + c}")
            def phiev(e, c=c):
                e.activation(phiq_b[c % 2], pqk_a, AF.Exp,
                             scale=rq2[:, 0:1], bias=lnS0[:, 0:1])
                return e.activation(phik_b[c % 2], pqk_b, AF.Exp,
                                    scale=rq2[:, 1:2], bias=lnS0[:, 1:2])
            add("S", phiev, waits=tuple(wts), key=f"phiev{st}_{c}")
            wts = [f"phiev{st}_{c}"]
            if c >= 2:
                wts.append(f"smm{st}_{c - 2}")
            elif st > 0:
                wts.append(f"smm{st - 1}_{6 + c}")
            add("V", lambda e, c=c: e.tensor_tensor(
                p_b[c % 2], phiq_b[c % 2], phik_b[c % 2], ALU.mult),
                waits=tuple(wts), key=f"pmul{st}_{c}")
            wts = [f"pmul{st}_{c}", "ones_s"]
            if st > 0 and c == 0:
                wts.append(f"ecp{st - 1}")
            add("T", lambda e, c=c: e.matmul(
                p_s, ones_s[:, 16 * c:16 * (c + 1)], p_b[c % 2],
                start=(c == 0), stop=(c == 7)),
                waits=tuple(wts), key=f"smm{st}_{c}")
        add("V", lambda e: e.reciprocal(e_rc, p_s), waits=(f"smm{st}_7",),
            key=f"erc{st}", fence=True)
        add("V", lambda e: e.tensor_tensor(e_fm, p_s, e_rc, ALU.mult),
            waits=(f"erc{st}",), key=f"e{st}")
        def etr(e, st=st):
            last = None
            for t in range(4):
                last = e.matmul(p_misc[:, 64 + 16 * t: 64 + 16 * (t + 1)],
                                e_fm[:, 128 * t:128 * (t + 1)],
                                id16_f32, start=True, stop=True)
            return last
        add("T", etr, waits=(f"e{st}", "idf"), key=f"etr{st}")
        add("V", lambda e: e.tensor_copy(e_tm, p_misc[:, 64:128]),
            waits=(f"etr{st}",), key=f"ecp{st}", fence=True)
        for t in range(4):
            g = st * 4 + t
            xg = x_sb[:, g * D:(g + 1) * D]
            add("S", lambda e, xg=xg, t=t, g=g: e.activation(
                xc_b[t % 2], xg, AF.Identity, bias=negm_s[:, g:g + 1]),
                waits=(f"lnTmmB{g - 2}",) if g >= 2 else ())
            add("S", lambda e, t=t, g=g: e.activation(
                diag_b[t % 2], id_bf, AF.Copy, scale=rstd_s[:, g:g + 1]),
                key=f"xcdB{g}")
            def lntmm(e, t=t):
                last = None
                for c in range(8):
                    last = e.matmul(pool_a[:, 128 * c:128 * (c + 1)],
                                    xc_b[t % 2][:, 128 * c:128 * (c + 1)],
                                    diag_b[t % 2], start=True, stop=True)
                return last
            wts = [f"xcdB{g}"]
            wts.append(f"lnTevB{g - 1}" if g >= 1 else "lnTev15")
            add("T", lntmm, waits=tuple(wts), key=f"lnTmmB{g}")
            add("S", lambda e, t=t: e.activation(
                lnT_b[t % 2], pool_a, AF.Copy),
                waits=(f"lnTmmB{g}",)
                + (tuple(f"terndo{j}" for j in range(8)) if g <= 1 else ()),
                key=f"lnTevB{g}")
            def vmm(e, t=t):
                for c in range(8):
                    e.matmul(pool_b[:, 0:512],
                             lnT_b[t % 2][:, 128 * c:128 * (c + 1)],
                             wvT[:, c * D + 0: c * D + 512],
                             start=(c == 0), stop=False)
                    e.matmul(pool_b[:, 512:1024],
                             lnT_b[t % 2][:, 128 * c:128 * (c + 1)],
                             wvT[:, c * D + 512: c * D + 1024],
                             start=(c == 0), stop=False)
                e.matmul(pool_b[:, 0:512], ones1_bf, bv_bf[:, 0:512],
                         start=False, stop=True)
                return e.matmul(pool_b[:, 512:1024], ones1_bf, bv_bf[:, 512:1024],
                                start=False, stop=True)
            wts = [f"lnTevB{g}", "wv", "bvbf"]
            if g >= 1:
                wts.append(f"x1_{g - 1}")
            add("T", vmm, waits=tuple(wts), key=f"vmm{g}")
            add("V", lambda e, t=t: e.tensor_tensor(
                d_sc.rearrange("p (h w) -> p h w", h=16),
                pool_b.rearrange("p (h w) -> p h w", h=16),
                e_tm[:, 16 * t:16 * (t + 1)].unsqueeze(2).broadcast_to(
                    [128, 16, 64]),
                ALU.mult), waits=(f"vmm{g}", f"ecp{st}"), key=f"am{g}",
                fence=True)
            add("V", lambda e, xg=xg: e.tensor_tensor(xg, xg, d_sc, ALU.add),
                waits=(f"am{g}",), key=f"x1_{g}", fence=True)
    add("Q", lambda e: e.dma_start(out=had_sb, in_=dhad),
        waits=(f"phimm{NST - 1}_7",), dma=True, key="had")

    qT2_b = [lnT_st[:, 0:1024], lnT_st[:, 1024:2048]]
    qbf2_b = [lnT_st[:, 2048:3072], lnT_st[:, 3072:4096]]

    def stage1a(g):
        xg = x_sb[:, g * D:(g + 1) * D]
        bn2 = bn2_b[g % 2]
        add("V", lambda e, xg=xg, bn2=bn2, g=g: e.scalar_tensor_tensor(
            cjunk, xg, 1.0, xg, ALU.mult, ALU.max, accum_out=bn2[:, 0:1]),
            waits=(f"x1_{g}",)
            + ((f"xcdC{g - 2}",) if g >= 2 else (f"phimm{NST - 1}_7",)),
            key=f"sumC{g}", fence=True)
        add("V", lambda e, xg=xg, bn2=bn2: e.scalar_tensor_tensor(
            cjunk, xg, 1.0, xg, ALU.mult, ALU.mult, accum_out=bn2[:, 1:2]),
            key=f"ssC{g}", fence=True)
        add("V", lambda e, g=g, bn2=bn2: e.tensor_scalar_mul(
            negm_s[:, g:g + 1], bn2[:, 0:1], -1.0 / D), waits=(f"sumC{g}",),
            key=f"nmC{g}", fence=True)
        add("V", lambda e, g=g, bn2=bn2: e.tensor_tensor(
            bn6[:, 0:1], bn2[:, 0:1], negm_s[:, g:g + 1], ALU.mult),
            waits=(f"nmC{g}",), key=f"t2C{g}", fence=True)
        add("V", lambda e, bn2=bn2: e.tensor_tensor(
            bn2[:, 1:2], bn2[:, 1:2], bn6[:, 0:1], ALU.add),
            waits=(f"ssC{g}", f"t2C{g}"), key=f"stC{g}")
        add("S", lambda e, bn2=bn2: e.activation(
            stat1, bn2[:, 1:2], AF.Ln, bias=eps_c, scale=1.0 / D),
            waits=(f"stC{g}",))
        add("S", lambda e, g=g: e.activation(
            rstd_s[:, g:g + 1], stat1, AF.Exp, scale=-0.5))
        add("S", lambda e, xg=xg, g=g: e.activation(
            xc_b[g % 2], xg, AF.Identity, bias=negm_s[:, g:g + 1]),
            waits=(f"ln2mm{g - 2}",) if g >= 2 else ())
        add("S", lambda e, g=g: e.activation(
            diag_b[g % 2], id_bf, AF.Copy, scale=rstd_s[:, g:g + 1]),
            key=f"xcdC{g}")

        def ln2mm(e, g=g):
            last = None
            for c in range(8):
                last = e.matmul(pool_a[:, 128 * c:128 * (c + 1)],
                                xc_b[g % 2][:, 128 * c:128 * (c + 1)],
                                diag_b[g % 2], start=True, stop=True)
            return last
        wts = [f"xcdC{g}"]
        if g == 0:
            wts.append("lnTevB15")
        elif g <= 2:
            wts.append(f"ln2ev{g - 1}")
        else:
            wts.append(f"qTev{g - 3}")
        add("T", ln2mm, waits=tuple(wts), key=f"ln2mm{g}")
        add("S", lambda e, g=g: e.activation(lnT_b[g % 2], pool_a, AF.Copy),
            waits=(f"ln2mm{g}",) + ((f"xhmm{g - 2}",) if g >= 2 else ()),
            key=f"ln2ev{g}")

    def stage1b(g):
        def xhmm(e, g=g):
            last = None
            for c in range(8):
                e.matmul(pool_b[:, 0:512],
                         lnT_b[g % 2][:, 128 * c:128 * (c + 1)],
                         had_sb[:, c * D + 0: c * D + 512],
                         start=(c == 0), stop=(c == 7))
                last = e.matmul(pool_b[:, 512:1024],
                                lnT_b[g % 2][:, 128 * c:128 * (c + 1)],
                                had_sb[:, c * D + 512: c * D + 1024],
                                start=(c == 0), stop=(c == 7))
            return last
        wts = [f"ln2ev{g}", "had"]
        if g >= 1:
            wts += [f"dq{g - 1}", f"rm{g - 1}"]
        else:
            wts.append("x1_15")
        add("T", xhmm, waits=tuple(wts), key=f"xhmm{g}")

    def stage2a(g):
        add("V", lambda e: e.tensor_reduce(
            rm_c, pool_b, axis=AX.X, op=ALU.max, apply_absolute_value=True),
            waits=(f"xhmm{g}",), key=f"rm{g}", fence=True)
        add("V", lambda e: e.tensor_scalar_max(cc_c, rm_c, 1e-5),
            waits=(f"rm{g}",), key=f"cc{g}", fence=True)
        add("V", lambda e: e.reciprocal(rm_c, cc_c),
            waits=(f"cc{g}",), key=f"rcc{g}", fence=True)
        add("V", lambda e, g=g: e.tensor_scalar_mul(
            rs_b[g % 2], rm_c, 7.0),
            waits=(f"rcc{g}",) + ((f"dq{g - 2}",) if g >= 2 else ()),
            key=f"rs{g}", fence=True)
        add("V", lambda e, g=g: e.tensor_scalar_mul(
            scale_b[g % 2], cc_c, -1.0 / 7.0),
            waits=(f"cc{g}",), key=f"qsc{g}", fence=True)
        add("S", lambda e, g=g: e.activation(
            d_b[g % 2], pool_b, AF.Copy, scale=rs_b[g % 2]),
            waits=(f"rs{g}",) + ((f"qc{g - 2}",) if g >= 2 else ()),
            key=f"dq{g}")

    def stage2b(g):
        xg = x_sb[:, g * D:(g + 1) * D]
        add("S", lambda e, g=g: e.copy(qi8_b[g % 2], d_b[g % 2]))
        add("S", lambda e, g=g: e.copy(qbf2_b[g % 2], qi8_b[g % 2]),
            waits=(f"qtmm{g - 2}",) if g >= 2 else (), key=f"qb{g}")
        add("V", lambda e, g=g: e.tensor_tensor(
            sg_sc, d_b[g % 2], qbf2_b[g % 2], ALU.subtract),
            waits=(f"qb{g}",), key=f"qc{g}", fence=True)
        add("V", lambda e, g=g: e.scalar_tensor_tensor(
            qi8_b[g % 2], sg_sc, 0.5, qbf2_b[g % 2], ALU.is_gt, ALU.add),
            waits=(f"qc{g}",), key=f"qe{g}", fence=True)
        add("V", lambda e, g=g: e.scalar_tensor_tensor(
            sg_bf, sg_sc, -0.5, qi8_b[g % 2], ALU.is_lt, ALU.subtract),
            waits=(f"qe{g}",), key=f"qf{g}", fence=True)
        add("V", lambda e, g=g: e.tensor_scalar(
            qbf2_b[g % 2], sg_bf, -7.0, 8.0, ALU.max, ALU.min),
            waits=(f"qf{g}",), key=f"qbf{g}")

        def qtmm(e, g=g):
            last = None
            for c in range(8):
                last = e.matmul(pool_a[:, 128 * c:128 * (c + 1)],
                                qbf2_b[g % 2][:, 128 * c:128 * (c + 1)],
                                id_bf, start=True, stop=True)
            return last
        add("T", qtmm,
            waits=(f"qbf{g}",
                   f"ln2ev{g + 2}" if g < NT - 2 else f"qTev{g - 1}"),
            key=f"qtmm{g}")
        add("S", lambda e, g=g: e.activation(qT2_b[g % 2], pool_a, AF.Copy),
            waits=(f"qtmm{g}",) + ((f"omm{g - 2}",) if g >= 2 else ()),
            key=f"qTev{g}")

        def omm(e, g=g):
            last = None
            for c in range(8):
                e.matmul(pqk_a,
                         qT2_b[g % 2][:, 128 * c:128 * (c + 1)],
                         ternT[:, c * D + 0: c * D + 512],
                         start=(c == 0), stop=(c == 7))
                last = e.matmul(pqk_b,
                                qT2_b[g % 2][:, 128 * c:128 * (c + 1)],
                                ternT[:, c * D + 512: c * D + 1024],
                                start=(c == 0), stop=(c == 7))
            return last
        wts = [f"qTev{g}", f"dq{g}", f"qsc{g}", "ternld"]
        if g >= 1:
            wts.append(f"x2_{g - 1}")
        else:
            wts.append("phiev3_7")
        add("T", omm, waits=tuple(wts), key=f"omm{g}")
        add("V", lambda e, xg=xg, g=g: e.scalar_tensor_tensor(
            xg[:, 0:512], pqk_a, scale_b[g % 2], xg[:, 0:512],
            ALU.mult, ALU.add),
            waits=(f"omm{g}", f"qsc{g}"), key=f"x2a_{g}")
        add("V", lambda e, xg=xg, g=g: e.scalar_tensor_tensor(
            xg[:, 512:1024], pqk_b, scale_b[g % 2], xg[:, 512:1024],
            ALU.mult, ALU.add), key=f"x2_{g}")

    stage1a(0)
    stage1a(1)
    stage1b(0)
    stage2a(0)
    stage1a(2)
    stage1b(1)
    for g in range(NT):
        if g + 1 < NT:
            stage2a(g + 1)
        if g + 2 < NT:
            stage1b(g + 2)
        stage2b(g)
        if g + 3 < NT:
            stage1a(g + 3)
        if g % 4 == 3:
            st = g // 4
            add("Q", lambda e, st=st: e.dma_start(
                out=dout[:, st * 4 * D:(st + 1) * 4 * D],
                in_=x_sb[:, st * 4 * D:(st + 1) * 4 * D]),
                waits=tuple(f"x2a_{st * 4 + t}" for t in range(4))
                + tuple(f"x2_{st * 4 + t}" for t in range(4)),
                dma=True, key=f"store{st}")

    def final_wait(e):
        for st in range(NST):
            e.wait_ge(sch.dma_sem(f"store{st}"), 16)
        return None
    add("Q", final_wait)

    with nc.Block() as block:
        @block.tensor
        def _(t):
            sch.emit_engine("T", t)

        @block.vector
        def _(v):
            sch.emit_engine("V", v)

        @block.scalar
        def _(s):
            sch.emit_engine("S", s)

        @block.sync
        def _(q):
            sch.emit_engine("Q", q)

        @block.gpsimd
        def _(g):
            sch.emit_engine("G", g)
    return nc


_HAD = None
def _hadamard_bf16():
    global _HAD
    if _HAD is None:
        h = np.array([[1.0]], dtype=np.float64)
        while h.shape[0] < D:
            h = np.block([[h, h], [h, -h]])
        _HAD = (h / np.sqrt(D)).astype(np.float32).astype(ml_dtypes.bfloat16)
    return _HAD


def _sb_layout(w):
    n = w.shape[1]
    return np.ascontiguousarray(
        w.reshape(8, 128, n).transpose(1, 0, 2).reshape(128, 8 * n))


def make_inputs_common(Wq, bq, Wk, bk, Wv, bv, weights, wffn):
    wgt = np.asarray(weights, np.float32)
    wblk = np.zeros((8, 128, 128), np.float32)
    for c in range(8):
        for hh in range(2):
            wblk[c, 64 * hh:64 * (hh + 1), 64 * hh:64 * (hh + 1)] = wgt[2 * c + hh].T
    ones_s = np.zeros((128, 128), np.float32)
    for c in range(8):
        for hh in range(2):
            h = 2 * c + hh
            ones_s[64 * hh:64 * (hh + 1), 16 * c + h] = 1.0
    idm = np.eye(128, dtype=np.float32)
    return {
        "wqT": _sb_layout(np.ascontiguousarray(np.asarray(Wq, np.float32).T)),
        "wkT": _sb_layout(np.ascontiguousarray(np.asarray(Wk, np.float32).T)),
        "wvT": _sb_layout(np.ascontiguousarray(np.asarray(Wv, np.float32).T)),
        "bq": np.ascontiguousarray(np.asarray(bq, np.float32).reshape(8, 128).T),
        "bk": np.ascontiguousarray(np.asarray(bk, np.float32).reshape(8, 128).T),
        "bv": np.ascontiguousarray(np.asarray(bv, np.float32)),
        "wblk": np.ascontiguousarray(
            wblk.transpose(1, 0, 2).reshape(128, 8 * 128)),
        "wffn": np.ascontiguousarray(
            np.asarray(wffn, np.float32).reshape(8, 128, D)),
        "had": _sb_layout(_hadamard_bf16()),
        "ones_s": ones_s.astype(ml_dtypes.bfloat16),
        "id_bf": idm.astype(ml_dtypes.bfloat16),
        "id16_f32": np.eye(16, dtype=np.float32),
    }


_NC = None
def kernel(L0_attn_x, L0_attn_Wq, L0_attn_bq, L0_attn_Wk, L0_attn_bk,
           L0_attn_Wv, L0_attn_bv, L0_attn_weights, L0_ffn_weight):
    global _NC
    if _NC is None:
        _NC = build_program()
    nc = _NC
    x = np.asarray(L0_attn_x, dtype=np.float32)
    xl = np.ascontiguousarray(
        x.reshape(NCORES, NT, 128, D).transpose(0, 2, 1, 3).reshape(
            NCORES, 128, NT * D))
    common = make_inputs_common(L0_attn_Wq, L0_attn_bq, L0_attn_Wk, L0_attn_bk,
                                L0_attn_Wv, L0_attn_bv, L0_attn_weights,
                                L0_ffn_weight)
    in_maps = [dict(common, x=xl[b]) for b in range(NCORES)]
    res = run_bass_kernel_spmd(nc, in_maps, core_ids=list(range(NCORES)))
    out = np.stack([np.asarray(res.results[b]["out"]) for b in range(NCORES)],
                   axis=0)
    out = out.reshape(NCORES, 128, NT, D).transpose(0, 2, 1, 3).reshape(
        NCORES, T_TOK, D)
    return np.ascontiguousarray(out).astype(np.float32)
